# version 03
# baseline (speedup 1.0000x reference)
import numpy as np
from contextlib import ExitStack

import concourse.bass as bass
import concourse.mybir as mybir
from concourse.bass_utils import run_bass_kernel_spmd

F32 = mybir.dt.float32
BF16 = mybir.dt.bfloat16
F16 = mybir.dt.float16
I8 = mybir.dt.int8
AF = mybir.ActivationFunctionType
ALU = mybir.AluOpType
AX = mybir.AxisListType

T = 2048
TO = 1024
E = 2048
H = 8
FC = 16
NT = 16
NQ = 8
C_SCALE = float(np.float32(128.0 ** 0.5))
C23 = float(1.5 * 2.0 ** 23)
C52 = float(2.0 ** 52)

f32 = np.float32

STOP_PHASE = "CD"
NHEADS = H


class Sem:
    def __init__(self, ctx, nc, name):
        self.h = ctx.enter_context(nc.semaphore(name))
        self.n = 0
        self.name = name


class Shadow:

    def __init__(self):
        self.q = {e: [] for e in ("pe", "dve", "act", "sp", "gp")}

    def ev(self, eng, waits, incs):
        self.q[eng].append((list(waits), list(incs)))

    def check(self):
        counts = {}
        pos = {e: 0 for e in self.q}
        total = sum(len(v) for v in self.q.values())
        done = 0
        progress = True
        while done < total:
            if not progress:
                stuck = {e: (self.q[e][pos[e]][0] if pos[e] < len(self.q[e]) else None)
                         for e in self.q}
                raise RuntimeError(f"schedule deadlock; counts={counts}; heads={stuck}")
            progress = False
            for e in self.q:
                while pos[e] < len(self.q[e]):
                    waits, incs = self.q[e][pos[e]]
                    if all(counts.get(s, 0) >= v for s, v in waits):
                        for s, a in incs:
                            counts[s] = counts.get(s, 0) + a
                        pos[e] += 1
                        done += 1
                        progress = True
                    else:
                        break
        return counts


def build_program(stop_phase=None, nheads=None):
    stop_phase = stop_phase or STOP_PHASE
    NH = nheads or NHEADS
    nc = bass.Bass("TRN2", target_bir_lowering=False, debug=False,
                   detect_race_conditions=False)
    sh = Shadow()

    xb = nc.dram_tensor("xb", [T, E], F32, kind="ExternalInput").ap()
    wqkvT = nc.dram_tensor("wqkvT", [128, H, FC, 384], F32, kind="ExternalInput").ap()
    bqkvT = nc.dram_tensor("bqkvT", [128, H, 3], F32, kind="ExternalInput").ap()
    wpT_own = nc.dram_tensor("wpT_own", [1024, E], F32, kind="ExternalInput").ap()
    wpT_oth = nc.dram_tensor("wpT_oth", [1024, E], F32, kind="ExternalInput").ap()
    bp = nc.dram_tensor("bp", [E], F32, kind="ExternalInput").ap()
    wfT = nc.dram_tensor("wfT", [E, E], I8, kind="ExternalInput").ap()
    wsc = nc.dram_tensor("wsc", [E], F32, kind="ExternalInput").ap()
    fbias = nc.dram_tensor("fbias", [E], F32, kind="ExternalInput").ap()
    mAcol = nc.dram_tensor("mAcol", [128, 1], F32, kind="ExternalInput").ap()
    mBcol = nc.dram_tensor("mBcol", [128, 1], F32, kind="ExternalInput").ap()

    out_own = nc.dram_tensor("out_own", [TO, E], F32, kind="ExternalOutput").ap()

    own_dram = nc.dram_tensor("own_dram", [128, H, 1024], BF16, kind="Internal").ap()
    cbufA = nc.dram_tensor("cbufA", [2, 128, 4, 1024], BF16, kind="Internal").ap()
    cbufB = nc.dram_tensor("cbufB", [2, 128, 4, 1024], BF16, kind="Internal").ap()
    rbufA = nc.dram_tensor("rbufA", [2, 128, 4, 1024], BF16, kind="Internal").ap()
    rbufB = nc.dram_tensor("rbufB", [2, 128, 4, 1024], BF16, kind="Internal").ap()
    f16_dram = nc.dram_tensor("f16_dram", [TO, E], F16, kind="Internal").ap()
    rgroups = [[0, 1], [2, 3], [4, 5], [6, 7]]

    ctx = ExitStack()
    sem = lambda name: Sem(ctx, nc, name)

    qq_fm = ctx.enter_context(
        nc.sbuf_tensor("qq_fm", [128, FC, TO], BF16, is_transpose_dest=True))
    isc = ctx.enter_context(nc.sbuf_tensor("isc", [128, NQ], F32))
    mA = ctx.enter_context(nc.sbuf_tensor("mA", [128, 1], F32))
    mB = ctx.enter_context(nc.sbuf_tensor("mB", [128, 1], F32))

    s_xld = sem("xld"); s_sq = sem("sq"); s_var = sem("var"); s_sqv = sem("sqv")
    s_ln = sem("ln"); s_xnb = sem("xnb"); s_tpxn = sem("tpxn")
    s_qq = sem("qq"); s_tpqq = sem("tpqq"); s_msk = sem("msk")
    s_wld = sem("wld"); s_qkmm = sem("qkmm"); s_qkvv = sem("qkvv"); s_vtp = sem("vtp")
    s_smm = sem("smm"); s_sm4 = sem("sm4"); s_spass = sem("spass"); s_nb = sem("nb"); s_expv = sem("expv")
    s_pnv = sem("pnv"); s_rlv = sem("rlv"); s_ptp = [sem("ptp0"), sem("ptp1")]
    s_avm = sem("avm"); s_avev = sem("avev"); s_ownw = sem("ownw"); s_giftw = sem("giftw")
    s_phB = sem("phB")
    s_coll = sem("coll"); s_collA = sem("collA"); s_rld = sem("rld"); s_recvv = sem("recvv"); s_atld = sem("atld")
    s_bc = sem("bc"); s_bc2 = sem("bc2"); s_xotl = sem("xotl")
    s_wpl = sem("wpl"); s_pjm = sem("pjm")
    s_wfl = sem("wfl"); s_fnm = sem("fnm"); s_f1v = sem("f1v"); s_f2v = sem("f2v")
    s_f16w = sem("f16w"); s_f16r = sem("f16r")
    s_oe1 = sem("oe1"); s_outv = sem("outv"); s_outw = sem("outw")
    s_vch = sem("vch")
    s_wfp = sem("wfp")

    block = ctx.enter_context(nc.Block())

    def wt(eng, ename, *conds):
        for s, v in conds:
            if v > 0:
                eng.wait_ge(s.h, v)
        sh.ev(ename, [(s.name, v) for s, v in conds if v > 0], [])

    def ti(instr, ename, s, amt=1):
        instr.then_inc(s.h, amt)
        sh.ev(ename, [], [(s.name, amt)])

    def dmaS(eng, ename, s, out, in_, transpose=False):
        if s.n > 0:
            eng.wait_ge(s.h, s.n)
        waits = [(s.name, s.n)] if s.n > 0 else []
        if transpose:
            eng.dma_start_transpose(out, in_).then_inc(s.h, 16)
        else:
            eng.dma_start(out=out, in_=in_).then_inc(s.h, 16)
        s.n += 16
        sh.ev(ename, waits, [(s.name, 16)])

    vch_n = [0]

    def vinc(instr):
        vch_n[0] += 1
        instr.then_inc(s_vch.h, 1)
        sh.ev("dve", [], [("vch", 1)])

    def vbar(v):
        v.wait_ge(s_vch.h, vch_n[0])
        sh.ev("dve", [("vch", vch_n[0])], [])

    A_TPXN = 16 * NT
    A_TPQQ = 16 * NQ

    esAB = ExitStack()
    xn_fm = esAB.enter_context(
        nc.sbuf_tensor("xn_fm", [128, FC, T], BF16, is_transpose_dest=True))
    if True:

        esA = ExitStack()
        xtile = esA.enter_context(nc.sbuf_tensor("xtile", [128, 2, E], F32))
        sqscr = esA.enter_context(nc.sbuf_tensor("sqscr", [128, E], F32))
        sxbf = esA.enter_context(nc.sbuf_tensor("sxbf", [128, E], BF16))
        xn_tm = esA.enter_context(nc.sbuf_tensor("xn_tm", [128, E], F32))
        xnbf = esA.enter_context(nc.sbuf_tensor("xnbf", [128, 2, E], BF16))
        qqst = esA.enter_context(nc.sbuf_tensor("qqst", [128, 2, E], BF16))
        r0 = esA.enter_context(nc.sbuf_tensor("r0", [128, 1024], F32))
        r1 = esA.enter_context(nc.sbuf_tensor("r1", [128, 1024], F32))
        r2 = esA.enter_context(nc.sbuf_tensor("r2", [128, 1024], F32))
        r3 = esA.enter_context(nc.sbuf_tensor("r3", [128, 1024], F32))
        sxc = esA.enter_context(nc.sbuf_tensor("sxc", [128, NT], F32))
        sxxc = esA.enter_context(nc.sbuf_tensor("sxxc", [128, NT], F32))
        negm = esA.enter_context(nc.sbuf_tensor("negm", [128, NT], F32))
        varb = esA.enter_context(nc.sbuf_tensor("varb", [128, NT], F32))
        sdev = esA.enter_context(nc.sbuf_tensor("sdev", [128, NT], F32))
        rstd = esA.enter_context(nc.sbuf_tensor("rstd", [128, NT], F32))
        amb = esA.enter_context(nc.sbuf_tensor("amb", [128, 4], F32))
        if True:
            def a_sync(s):
                dmaS(s, "sp", s_msk, mA[:, :], mAcol[:, :])
                dmaS(s, "sp", s_msk, mB[:, :], mBcol[:, :])
                for i in range(NT):
                    if i >= 2:
                        wt(s, "sp", (s_sq, i - 1), (s_ln, i - 1))
                    dmaS(s, "sp", s_xld, xtile[:, i % 2, :],
                         xb[128 * i:128 * (i + 1), :])
                    if i >= 1:
                        j = i - 1
                        wt(s, "sp", (s_xnb, j + 1))
                        dmaS(s, "sp", s_tpxn, xn_fm[:, :, 128 * j:128 * (j + 1)],
                             xnbf[:, j % 2, :], transpose=True)
                wt(s, "sp", (s_xnb, NT))
                dmaS(s, "sp", s_tpxn, xn_fm[:, :, 128 * (NT - 1):128 * NT],
                     xnbf[:, (NT - 1) % 2, :], transpose=True)

            def a_act(sc):
                for i in range(NT + 1):
                    if i < NT:
                        wt(sc, "act", (s_xld, 16 * (i + 1)), (s_sq, i))
                        sc.activation(sxbf[:, :], xtile[:, i % 2, :], AF.Identity,
                                      bias=f32(0.0), scale=f32(1.0),
                                      accum_out=sxc[:, i:i + 1])
                        ti(sc.activation(sqscr[:, :], xtile[:, i % 2, :], AF.Square,
                                         bias=f32(0.0), scale=f32(1.0),
                                         accum_out=sxxc[:, i:i + 1]),
                           "act", s_sq)
                    if i >= 1:
                        j = i - 1
                        wt(sc, "act", (s_var, j + 1))
                        ti(sc.activation(sdev[:, j:j + 1], varb[:, j:j + 1], AF.Sqrt,
                                         bias=f32(1e-5), scale=f32(1.0)),
                           "act", s_sqv)
                        if j < NQ:
                            wt(sc, "act", (s_qq, j + 1))
                            dmaS(sc, "act", s_tpqq,
                                 qq_fm[:, :, 128 * j:128 * (j + 1)],
                                 qqst[:, j % 2, :], transpose=True)

            def a_dve(v):
                for i in range(NT):
                    wt(v, "dve", (s_xld, 16 * (i + 1)), (s_sq, i + 1))
                    vinc(v.tensor_scalar(negm[:, i:i + 1], sxc[:, i:i + 1],
                                         -1.0 / E, None, ALU.mult))
                    v.tensor_scalar(varb[:, i:i + 1], sxxc[:, i:i + 1],
                                    1.0 / E, None, ALU.mult)
                    vbar(v)
                    vinc(v.tensor_mul(r3[:, 0:1], negm[:, i:i + 1], negm[:, i:i + 1]))
                    vbar(v)
                    ti(v.tensor_sub(varb[:, i:i + 1], varb[:, i:i + 1], r3[:, 0:1]),
                       "dve", s_var)
                    wt(v, "dve", (s_sqv, i + 1))
                    vinc(v.reciprocal(rstd[:, i:i + 1], sdev[:, i:i + 1]))
                    vbar(v)
                    ti(v.tensor_scalar(xn_tm[:, :], xtile[:, i % 2, :],
                                       negm[:, i:i + 1], rstd[:, i:i + 1],
                                       ALU.add, ALU.mult),
                       "dve", s_ln)
                    if i >= 2:
                        wt(v, "dve", (s_tpxn, 16 * (i - 1)))
                    ti(v.tensor_scalar(xnbf[:, i % 2, :], xn_tm[:, :], 1.0,
                                       None, ALU.mult),
                       "dve", s_xnb)
                    if i < NQ:
                        vinc(v.reduce_max(amb[:, 0:1], xn_tm[:, :], axis=AX.X,
                                          apply_absolute_value=True))
                        vbar(v)
                        vinc(v.tensor_scalar(amb[:, 1:2], amb[:, 0:1], 1e-7,
                                             None, ALU.max))
                        vbar(v)
                        vinc(v.reciprocal(amb[:, 2:3], amb[:, 1:2]))
                        vbar(v)
                        vinc(v.tensor_scalar(amb[:, 3:4], amb[:, 2:3], 127.0,
                                             None, ALU.mult))
                        v.tensor_scalar(isc[:, i:i + 1], amb[:, 1:2],
                                        1.0 / 127.0, None, ALU.mult)
                        vbar(v)
                        if i >= 2:
                            wt(v, "dve", (s_tpqq, 16 * (i - 1)))
                        for hh in range(2):
                            sl = slice(1024 * hh, 1024 * (hh + 1))
                            v.tensor_scalar(r0[:, :], xn_tm[:, sl], amb[:, 3:4],
                                            None, ALU.mult)
                            v.tensor_scalar(r1[:, :], r0[:, :], 256.0, C23,
                                            ALU.add, ALU.add)
                            v.tensor_scalar(r1[:, :], r1[:, :], C23, None,
                                            ALU.subtract)
                            v.tensor_scalar(r0[:, :], r1[:, :], C52, C52,
                                            ALU.add, ALU.subtract)
                            v.tensor_scalar(r2[:, :], r0[:, :], 0.0, None,
                                            ALU.is_equal)
                            v.tensor_mul(r3[:, :], r2[:, :], r1[:, :])
                            qs = v.scalar_tensor_tensor(qqst[:, i % 2, sl],
                                                        r0[:, :], -256.0, r3[:, :],
                                                        ALU.add, ALU.add)
                            if hh == 1:
                                ti(qs, "dve", s_qq)

            block.sync(a_sync)
            block.scalar(a_act)
            block.vector(a_dve)
            esA.close()

        if stop_phase == "A":
            def dbg_out(g):
                wt(g, "gp", (s_ln, NT), (s_tpxn, A_TPXN), (s_tpqq, A_TPQQ))
                for j in range(4):
                    dmaS(g, "gp", s_outw, out_own[128 * j:128 * (j + 1), :],
                         xn_fm[:, j, :])
                for j in range(4):
                    dmaS(g, "gp", s_outw,
                         out_own[128 * (4 + j):128 * (5 + j), 0:1024],
                         qq_fm[:, j, 0:1024])
                for j in range(4):
                    dmaS(g, "gp", s_outw,
                         out_own[128 * (4 + j):128 * (5 + j), 1024:2048],
                         qq_fm[:, 4 + j, 0:1024])
                wt(g, "gp", (s_outw, s_outw.n))
            block.gpsimd(dbg_out)

        if stop_phase != "A":
            esB = ExitStack()
            wqkv_sb = esB.enter_context(nc.sbuf_tensor("wqkv_sb", [128, 2, FC, 384], BF16))
            q_fm = esB.enter_context(nc.sbuf_tensor("q_fm", [128, 2, T], BF16))
            k_fm = esB.enter_context(nc.sbuf_tensor("k_fm", [128, 2, T], BF16))
            v_fm = esB.enter_context(nc.sbuf_tensor("v_fm", [128, 2, T], BF16))
            v_tm = esB.enter_context(nc.sbuf_tensor("v_tm", [128, 2, FC, 128], BF16, is_transpose_dest=True))
            S_sb = esB.enter_context(nc.sbuf_tensor("S_sb", [128, 2, T], F32))
            P_sb = esB.enter_context(nc.sbuf_tensor("P_sb", [128, 2, 2, T], BF16))
            PT = esB.enter_context(nc.sbuf_tensor("PT", [128, 2, 2, FC, 128], BF16, is_transpose_dest=True))
            bqs = esB.enter_context(nc.sbuf_tensor("bqs", [128, 2, 3], F32))
            smax = esB.enter_context(nc.sbuf_tensor("smax", [128, 2], F32))
            m4 = esB.enter_context(nc.sbuf_tensor("m4", [128, 2, 4], F32))
            nbias = esB.enter_context(nc.sbuf_tensor("nbias", [128, 2], F32))
            lrow = esB.enter_context(nc.sbuf_tensor("lrow", [128, 2], F32))
            rlb = esB.enter_context(nc.sbuf_tensor("rlb", [128, 2], F32))
            scrA = esB.enter_context(nc.sbuf_tensor("scrA", [128, 2], F32))
            atst = esB.enter_context(nc.sbuf_tensor("atst", [128, 2, 256], BF16))
            gstA = esB.enter_context(nc.sbuf_tensor("gstA", [128, 2, 256], BF16))
            gstB = esB.enter_context(nc.sbuf_tensor("gstB", [128, 2, 256], BF16))
            S_ps = esB.enter_context(nc.psum_tensor("S_ps", [128, T], F32))
            qkv_ps = esB.enter_context(nc.psum_tensor("qkv_ps", [128, 1024], F32))
            av_ps0 = esB.enter_context(nc.psum_tensor("av_ps0", [128, 512], F32))
            av_ps1 = esB.enter_context(nc.psum_tensor("av_ps1", [128, 512], F32))
            if True:
                av_ps = [av_ps0, av_ps1]

                def rec_phB(e):
                    sh.ev(e, [], [("phB", 1)])

                def wdma(g, h):
                    if h >= 2:
                        wt(g, "gp", (s_qkvv, 6 * (h - 1)))
                    dmaS(g, "gp", s_wld, wqkv_sb[:, h % 2, :, :], wqkvT[:, h, :, :])
                    dmaS(g, "gp", s_wld, bqs[:, h % 2, :], bqkvT[:, h, :])

                def b_gp(g):
                    wt(g, "gp", (s_ln, NT), (s_sqv, NT), (s_qq, NQ),
                       (s_tpxn, A_TPXN), (s_tpqq, A_TPQQ))
                    wdma(g, 0)
                    wdma(g, 1)
                    for h in range(NH):
                        if h == 4 and NH == H:
                            cc = g.collective_compute(
                                "AllReduce", ALU.add, rgroups,
                                ins=[cbufA[:, :, :, :]], outs=[rbufA[:, :, :, :]])
                            ti(cc, "gp", s_collA)
                        if h + 2 < NH:
                            wdma(g, h + 2)
                        for gq in range(8):
                            gidx = h * 8 + gq
                            wt(g, "gp", (s_avev, gidx + 1))
                            if gq < 4:
                                dmaS(g, "gp", s_ownw,
                                     own_dram[:, h, 256 * gq:256 * (gq + 1)],
                                     atst[:, gq % 2, :])
                            else:
                                gg = gq - 4
                                cb = cbufA if h < 4 else cbufB
                                hh2 = h % 4
                                dmaS(g, "gp", s_giftw,
                                     cb[0, :, hh2, 256 * gg:256 * (gg + 1)],
                                     gstA[:, gq % 2, :])
                                dmaS(g, "gp", s_giftw,
                                     cb[1, :, hh2, 256 * gg:256 * (gg + 1)],
                                     gstB[:, gq % 2, :])
                    g.engine_nop().then_inc(s_phB.h, 1)
                    rec_phB("gp")

                def qkv_unit(t, h, u, half=None):
                    oc, th = divmod(u, 2)
                    osl = slice(128 * oc, 128 * (oc + 1))
                    toff = 1024 * th
                    halves = (0, 1) if half is None else (half,)
                    uidx = 6 * h + u
                    if halves[0] == 0:
                        wt(t, "pe", (s_wld, 32 * (h + 1)))
                        if h == 0:
                            wt(t, "pe", (s_tpxn, 16 * 8 if th == 0 else A_TPXN))
                        if uidx >= 1:
                            wt(t, "pe", (s_qkvv, uidx))
                    for hf in halves:
                        for fc in range(8 * hf, 8 * hf + 8):
                            st = fc == 0
                            sp = fc == FC - 1
                            t.matmul(qkv_ps[:, 0:512], wqkv_sb[:, h % 2, fc, osl],
                                     xn_fm[:, fc, toff:toff + 512],
                                     start=st, stop=sp)
                            mm = t.matmul(qkv_ps[:, 512:1024],
                                          wqkv_sb[:, h % 2, fc, osl],
                                          xn_fm[:, fc, toff + 512:toff + 1024],
                                          start=st, stop=sp)
                    if halves[-1] == 1:
                        ti(mm, "pe", s_qkmm)

                def s_unit(t, h, qt):
                    u = h * NT + qt
                    wt(t, "pe", (s_qkvv, 6 * h + 4))
                    if u >= 1:
                        wt(t, "pe", (s_spass, u))
                    for j in range(4):
                        mm = t.matmul(S_ps[:, 512 * j:512 * (j + 1)],
                                      q_fm[:, h % 2, 128 * qt:128 * (qt + 1)],
                                      k_fm[:, h % 2, 512 * j:512 * (j + 1)],
                                      start=True, stop=True)
                        ti(mm, "pe", s_sm4)

                def av_unit(t, h, gq):
                    gidx = h * 8 + gq
                    wt(t, "pe", (s_ptp[gidx % 2], 16 * (gidx // 2 + 1)),
                       (s_qkvv, 6 * h + 6), (s_vtp, 16 * (h + 1)))
                    if gidx >= 2:
                        wt(t, "pe", (s_avev, gidx - 1))
                    for kc in range(FC):
                        mm = t.matmul(av_ps[gq % 2][:, 0:256], v_tm[:, h % 2, kc, :],
                                      PT[:, gidx % 2, :, kc, :],
                                      start=(kc == 0), stop=(kc == FC - 1))
                    ti(mm, "pe", s_avm)

                def b_pe(t):
                    for h in range(NH):
                        if h == 0:
                            for u in range(6):
                                qkv_unit(t, 0, u)
                        for qt in range(NT):
                            s_unit(t, h, qt)
                            if h + 1 < NH and 1 <= qt <= 12:
                                qkv_unit(t, h + 1, (qt - 1) // 2, half=(qt - 1) % 2)
                            if qt >= 6 and qt % 2 == 0:
                                av_unit(t, h, (qt - 6) // 2)
                        for gq in (5, 6, 7):
                            av_unit(t, h, gq)
                    t.nop().then_inc(s_phB.h, 1)
                    rec_phB("pe")

                def qkv_evac(v, h, u):
                    oc, th = divmod(u, 2)
                    toff = 1024 * th
                    dst = [q_fm, k_fm, v_fm][oc]
                    uidx = 6 * h + u
                    wt(v, "dve", (s_qkmm, uidx + 1))
                    if h >= 2:
                        if oc < 2:
                            wt(v, "dve", (s_sm4, 64 * (h - 1)))
                        else:
                            wt(v, "dve", (s_vtp, 16 * (h - 1)))
                    ti(v.tensor_scalar(dst[:, h % 2, toff:toff + 1024], qkv_ps[:, :],
                                       bqs[:, h % 2, oc:oc + 1], None, ALU.add),
                       "dve", s_qkvv)

                def s_pass(v, h, qt):
                    u = h * NT + qt
                    if u >= 2:
                        wt(v, "dve", (s_expv, u - 1))
                    for j in range(4):
                        wt(v, "dve", (s_sm4, 4 * u + j + 1))
                        tsj = v.tensor_scalar(S_sb[:, qt % 2, 512 * j:512 * (j + 1)],
                                              S_ps[:, 512 * j:512 * (j + 1)], C_SCALE,
                                              None, ALU.mult, ALU.max,
                                              accum_out=m4[:, qt % 2, j:j + 1])
                        if j == 3:
                            vinc(tsj)
                    vbar(v)
                    ti(v.reduce_max(nbias[:, qt % 2:qt % 2 + 1],
                                    m4[:, qt % 2, :], axis=AX.X, negate=True),
                       "dve", s_spass)


                def rl_pass(v, u):
                    qt = u % NT
                    wt(v, "dve", (s_expv, u + 1))
                    ti(v.reciprocal(rlb[:, qt % 2:qt % 2 + 1],
                                    lrow[:, qt % 2:qt % 2 + 1]),
                       "dve", s_rlv)

                def av_evac(v, h, gq):
                    gidx = h * 8 + gq
                    wt(v, "dve", (s_avm, gidx + 1))
                    if gq < 4:
                        own_idx = 4 * h + gq
                        if own_idx >= 2:
                            wt(v, "dve", (s_ownw, 16 * (own_idx - 1)))
                        ti(v.tensor_copy(atst[:, gq % 2, :], av_ps[gq % 2][:, 0:256]),
                           "dve", s_avev)
                    else:
                        gidx2 = 4 * h + (gq - 4)
                        if gidx2 >= 2:
                            wt(v, "dve", (s_giftw, 32 * (gidx2 - 1)))
                        v.tensor_scalar(gstA[:, gq % 2, :], av_ps[gq % 2][:, 0:256],
                                        mA[:, 0:1], None, ALU.mult)
                        ti(v.tensor_scalar(gstB[:, gq % 2, :],
                                           av_ps[gq % 2][:, 0:256],
                                           mB[:, 0:1], None, ALU.mult),
                           "dve", s_avev)

                def b_dve(v):
                    wt(v, "dve", (s_msk, 32))
                    for h in range(NH):
                        if h == 0:
                            for u in range(6):
                                qkv_evac(v, 0, u)
                        for qt in range(NT):
                            s_pass(v, h, qt)
                            u = h * NT + qt
                            if qt >= 1:
                                rl_pass(v, u - 1)
                            if h + 1 < NH and qt in (3, 5, 7, 9, 11, 13):
                                qkv_evac(v, h + 1, (qt - 3) // 2)
                            if qt >= 7 and qt % 2 == 1:
                                av_evac(v, h, (qt - 7) // 2)
                        rl_pass(v, h * NT + NT - 1)
                        for gq in (5, 6, 7):
                            av_evac(v, h, gq)
                    v.engine_nop().then_inc(s_phB.h, 1)
                    rec_phB("dve")

                def pnorm(sc, u):
                    qt = u % NT
                    wt(sc, "act", (s_rlv, u + 1))
                    ti(sc.mul(P_sb[:, (qt // 2) % 2, qt % 2, :],
                              P_sb[:, (qt // 2) % 2, qt % 2, :],
                              rlb[:, qt % 2:qt % 2 + 1]),
                       "act", s_pnv)

                def b_act(sc):
                    sc.memzero(scrA[:, :])
                    for h in range(NH):
                        for qt in range(NT):
                            u = h * NT + qt
                            gidx = h * 8 + qt // 2
                            wt(sc, "act", (s_spass, u + 1))
                            if gidx >= 2:
                                wt(sc, "act", (s_ptp[gidx % 2], 16 * (gidx // 2)))

                            ti(sc.activation(P_sb[:, (qt // 2) % 2, qt % 2, :],
                                             S_sb[:, qt % 2, :], AF.Exp,
                                             bias=nbias[:, qt % 2:qt % 2 + 1],
                                             scale=f32(1.0),
                                             accum_out=lrow[:, qt % 2:qt % 2 + 1]),
                               "act", s_expv)
                            if qt >= 1:
                                pnorm(sc, u - 1)
                        pnorm(sc, h * NT + NT - 1)
                    ti(sc.copy(scrA[:, 1:2], scrA[:, 0:1]), "act", s_phB)

                def b_sync(s):
                    for h in range(NH):
                        wt(s, "sp", (s_qkvv, 6 * h + 6))
                        if h >= 2:
                            wt(s, "sp", (s_avm, 8 * (h - 1)))
                        dmaS(s, "sp", s_vtp, v_tm[:, h % 2, :, :],
                             v_fm[:, h % 2, :], transpose=True)
                        for gq in range(8):
                            gidx = h * 8 + gq
                            wt(s, "sp", (s_pnv, h * NT + 2 * gq + 2))
                            if gidx >= 2:
                                wt(s, "sp", (s_avm, gidx - 1))
                            dmaS(s, "sp", s_ptp[gidx % 2], PT[:, gidx % 2, :, :, :],
                                 P_sb[:, gq % 2, :, :].rearrange(
                                     "p a t -> p (a t)"), transpose=True)
                    s.sem_inc(s_phB.h, 1)
                    rec_phB("sp")

                block.gpsimd(b_gp)
                block.tensor(b_pe)
                block.vector(b_dve)
                block.scalar(b_act)
                block.sync(b_sync)
                esB.close()

    esAB.close()

    if stop_phase != "CD":
        if stop_phase == "B":
            def dbg_out2(s):
                wt(s, "sp", (s_phB, 5))
                for j in range(NQ):
                    dmaS(s, "sp", s_outw, out_own[128 * j:128 * (j + 1), :],
                         xb[128 * j:128 * (j + 1), :])
                wt(s, "sp", (s_outw, 16 * NQ))
            block.sync(dbg_out2)
        sh.check()
        ctx.close()
        return nc

    B_OWNW = 16 * 4 * NH
    B_GIFTW = 32 * 4 * NH
    B_SPASS = NH * NT
    B_AVEV = NH * 8
    B_QKVV = NH * 6

    esCD = ExitStack()
    at_sb = esCD.enter_context(nc.sbuf_tensor("at_sb", [128, H, 1024], BF16))
    r0_sb = esCD.enter_context(nc.sbuf_tensor("r0_sb", [128, H, 1024], BF16))
    w_sb = esCD.enter_context(nc.sbuf_tensor("w_sb", [128, FC, E], BF16))
    xot = esCD.enter_context(nc.sbuf_tensor("xot", [128, 2, E], F32))
    wbb = esCD.enter_context(nc.sbuf_tensor("wbb", [128, E], F32))
    bfb = esCD.enter_context(nc.sbuf_tensor("bfb", [128, E], F32))
    f16st = esCD.enter_context(nc.sbuf_tensor("f16st", [128, 2, E], F16))
    f16r = esCD.enter_context(nc.sbuf_tensor("f16r", [128, 2, E], F16))
    osb = esCD.enter_context(nc.sbuf_tensor("osb", [128, 2, E], F32))
    mm_ps0 = esCD.enter_context(nc.psum_tensor("mm_ps0", [128, E], F32))
    mm_ps1 = esCD.enter_context(nc.psum_tensor("mm_ps1", [128, E], F32))
    if True:
        mm_ps = [mm_ps0, mm_ps1]

        def cd_gp(g):
            wt(g, "gp", (s_giftw, B_GIFTW), (s_ownw, B_OWNW))
            cc = g.collective_compute("AllReduce", ALU.add, rgroups,
                                      ins=[cbufB[:, :, :, :]],
                                      outs=[rbufB[:, :, :, :]])
            ti(cc, "gp", s_coll)
            wt(g, "gp", (s_phB, 5))
            for ec in range(FC):
                dmaS(g, "gp", s_wfl, w_sb[:, ec, :],
                     wfT[128 * ec:128 * (ec + 1), :])
            wt(g, "gp", (s_collA, 1))
            dmaS(g, "gp", s_rld, r0_sb[:, 0:4, :], rbufA[0, :, :, :])
            dmaS(g, "gp", s_rld, at_sb[:, 0:4, :], rbufA[1, :, :, :])
            wt(g, "gp", (s_coll, 1))
            dmaS(g, "gp", s_rld, r0_sb[:, 4:8, :], rbufB[0, :, :, :])
            dmaS(g, "gp", s_rld, at_sb[:, 4:8, :], rbufB[1, :, :, :])
            wt(g, "gp", (s_recvv, 2))
            dmaS(g, "gp", s_atld, at_sb[:, :, :], own_dram[:, :, :])
            wt(g, "gp", (s_fnm, NQ))
            for ec in range(FC):
                wsrc = wpT_own if ec < H else wpT_oth
                r = ec % H
                dmaS(g, "gp", s_wpl, w_sb[:, ec, :], wsrc[128 * r:128 * (r + 1), :])


        def cd_dve(v):
            for j in range(NQ):
                wt(v, "dve", (s_fnm, j + 1), (s_bc, 32))
                if j >= 2:
                    wt(v, "dve", (s_f16w, 16 * (j - 1)))
                ti(v.scalar_tensor_tensor(osb[:, j % 2, :], mm_ps[j % 2][:, :],
                                          isc[:, j:j + 1], wbb[:, :],
                                          ALU.mult, ALU.mult),
                   "dve", s_f1v)
                ti(v.tensor_add(f16st[:, j % 2, :], osb[:, j % 2, :], bfb[:, :]),
                   "dve", s_f2v)
            wt(v, "dve", (s_rld, 32))
            v.tensor_scalar(r0_sb[:, 0:4, :], r0_sb[:, 0:4, :], mB[:, 0:1],
                            None, ALU.mult)
            ti(v.scalar_tensor_tensor(r0_sb[:, 0:4, :], at_sb[:, 0:4, :], mA[:, 0:1],
                                      r0_sb[:, 0:4, :], ALU.mult, ALU.add),
               "dve", s_recvv)
            wt(v, "dve", (s_rld, 64))
            v.tensor_scalar(r0_sb[:, 4:8, :], r0_sb[:, 4:8, :], mB[:, 0:1],
                            None, ALU.mult)
            ti(v.scalar_tensor_tensor(r0_sb[:, 4:8, :], at_sb[:, 4:8, :], mA[:, 0:1],
                                      r0_sb[:, 4:8, :], ALU.mult, ALU.add),
               "dve", s_recvv)
            for j in range(NQ):
                wt(v, "dve", (s_pjm, j + 1), (s_xotl, 16 * (j + 1)),
                   (s_f16w, 16 * (j + 1)))
                if j >= 2:
                    wt(v, "dve", (s_outw, 16 * (j - 1)))
                wt(v, "dve", (s_f16r, 16 * (j + 1)), (s_bc2, 16))
                ti(v.scalar_tensor_tensor(osb[:, j % 2, :], mm_ps[j % 2][:, :], 0.0,
                                          xot[:, j % 2, :], ALU.add, ALU.add),
                   "dve", s_oe1)
                v.tensor_add(osb[:, j % 2, :], osb[:, j % 2, :], wbb[:, :])
                ti(v.tensor_add(osb[:, j % 2, :], osb[:, j % 2, :],
                                f16r[:, j % 2, :]),
                   "dve", s_outv)

        def cd_pe(t):
            for j in range(NQ):
                wt(t, "pe", (s_phB, 5), (s_spass, B_SPASS), (s_avev, B_AVEV),
                   (s_qkvv, B_QKVV))
                if j >= 2:
                    wt(t, "pe", (s_f1v, j - 1))
                for ec in range(FC):
                    wt(t, "pe", (s_wfl, 16 * (ec + 1)))
                    st = ec == 0
                    sp = ec == FC - 1
                    for oq in range(4):
                        mm = t.matmul(mm_ps[j % 2][:, 512 * oq:512 * (oq + 1)],
                                      qq_fm[:, ec, 128 * j:128 * (j + 1)],
                                      w_sb[:, ec, 512 * oq:512 * (oq + 1)],
                                      start=st, stop=sp)
                ti(mm, "pe", s_fnm)
            for j in range(NQ):
                wt(t, "pe", (s_recvv, 2), (s_atld, 16), (s_f1v, j + 1))
                if j >= 2:
                    wt(t, "pe", (s_oe1, j - 1))
                for ec in range(FC):
                    wt(t, "pe", (s_wpl, 16 * (ec + 1)))
                    lhs = (at_sb[:, ec, 128 * j:128 * (j + 1)] if ec < H
                           else r0_sb[:, ec - H, 128 * j:128 * (j + 1)])
                    st = ec == 0
                    sp = ec == FC - 1
                    for oq in range(4):
                        mm = t.matmul(mm_ps[j % 2][:, 512 * oq:512 * (oq + 1)],
                                      lhs, w_sb[:, ec, 512 * oq:512 * (oq + 1)],
                                      start=st, stop=sp)
                ti(mm, "pe", s_pjm)

        def cd_sync(s):
            wt(s, "sp", (s_phB, 5))
            dmaS(s, "sp", s_bc, wbb[:, :], wsc[None, :].broadcast_to((128, E)))
            dmaS(s, "sp", s_bc, bfb[:, :], fbias[None, :].broadcast_to((128, E)))
            for j in range(NQ):
                wt(s, "sp", (s_f2v, j + 1))
                dmaS(s, "sp", s_f16w, f16_dram[128 * j:128 * (j + 1), :],
                     f16st[:, j % 2, :])
            wt(s, "sp", (s_f1v, NQ))
            dmaS(s, "sp", s_bc2, wbb[:, :], bp[None, :].broadcast_to((128, E)))
            for j in range(NQ + 2):
                if j < NQ:
                    if j >= 2:
                        wt(s, "sp", (s_oe1, j - 1))
                    dmaS(s, "sp", s_xotl, xot[:, j % 2, :],
                         xb[128 * j:128 * (j + 1), :])
                    wt(s, "sp", (s_f16w, 16 * (j + 1)))
                    if j >= 2:
                        wt(s, "sp", (s_outv, j - 1))
                    dmaS(s, "sp", s_f16r, f16r[:, j % 2, :],
                         f16_dram[128 * j:128 * (j + 1), :])
                if j >= 2:
                    jj = j - 2
                    wt(s, "sp", (s_outv, jj + 1))
                    dmaS(s, "sp", s_outw, out_own[128 * jj:128 * (jj + 1), :],
                         osb[:, jj % 2, :])
            wt(s, "sp", (s_outw, 16 * NQ))

        block.gpsimd(cd_gp)
        block.vector(cd_dve)
        block.tensor(cd_pe)
        block.sync(cd_sync)
        esCD.close()

    sh.check()
    ctx.close()
    return nc


_NC_CACHE = {}


def _get_nc():
    if "nc" not in _NC_CACHE:
        _NC_CACHE["nc"] = build_program()
    return _NC_CACHE["nc"]


def kernel(L0_attn_query, L0_attn_key, L0_attn_value, L0_attn_Wqkv, L0_attn_bqkv,
           L0_attn_Wproj, L0_attn_bproj, L0_ffn_weight, L0_ffn_w_scale, L0_ffn_bias):
    x = np.asarray(L0_attn_query)
    Wqkv = np.asarray(L0_attn_Wqkv)
    bqkv = np.asarray(L0_attn_bqkv)
    Wproj = np.asarray(L0_attn_Wproj)
    bproj = np.asarray(L0_attn_bproj)
    wf = np.asarray(L0_ffn_weight)
    wscale = np.asarray(L0_ffn_w_scale)
    fb = np.asarray(L0_ffn_bias)

    Wr = Wqkv.reshape(16, 384, 2048)
    br = bqkv.reshape(16, 384)
    WpT = np.ascontiguousarray(Wproj.T)
    wfT = np.ascontiguousarray(wf.T)

    in_maps = []
    for c in range(8):
        b, g = divmod(c, 2)
        own = x[b, 1024 * g:1024 * (g + 1)]
        oth = x[b, 1024 * (1 - g):1024 * (2 - g)]
        xb_rot = np.ascontiguousarray(np.concatenate([own, oth], axis=0))
        wh = Wr[8 * g:8 * g + 8]
        wqkvT_c = np.ascontiguousarray(
            wh.reshape(8, 384, FC, 128).transpose(3, 0, 2, 1))
        bq_c = np.ascontiguousarray(
            br[8 * g:8 * g + 8].reshape(8, 3, 128).transpose(2, 0, 1))
        in_maps.append({
            "xb": xb_rot,
            "wqkvT": wqkvT_c,
            "bqkvT": bq_c,
            "wpT_own": np.ascontiguousarray(WpT[1024 * g:1024 * (g + 1)]),
            "wpT_oth": np.ascontiguousarray(WpT[1024 * (1 - g):1024 * (2 - g)]),
            "bp": bproj,
            "wfT": wfT,
            "wsc": np.ascontiguousarray(wscale[:, 0]),
            "fbias": fb,
            "mAcol": np.full((128, 1), 1.0 if g == 0 else 0.0, np.float32),
            "mBcol": np.full((128, 1), 0.0 if g == 0 else 1.0, np.float32),
        })

    nc = _get_nc()
    res = run_bass_kernel_spmd(nc, in_maps, core_ids=list(range(8)))

    out = np.empty((4, 2048, 2048), np.float32)
    for c in range(8):
        b, g = divmod(c, 2)
        out[b, 1024 * g:1024 * (g + 1)] = res.results[c]["out_own"]
    return out


# version 06
# speedup vs baseline: 1.0903x; 1.0903x over previous
import numpy as np
from contextlib import ExitStack

import concourse.bass as bass
import concourse.mybir as mybir
from concourse.bass_utils import run_bass_kernel_spmd

F32 = mybir.dt.float32
BF16 = mybir.dt.bfloat16
F16 = mybir.dt.float16
I8 = mybir.dt.int8
AF = mybir.ActivationFunctionType
ALU = mybir.AluOpType
AX = mybir.AxisListType

T = 2048
TO = 1024
E = 2048
H = 8
FC = 16
NT = 16
NQ = 8
C_SCALE = float(np.float32(128.0 ** 0.5))
C23 = float(1.5 * 2.0 ** 23)
C52 = float(2.0 ** 52)

f32 = np.float32

STOP_PHASE = "CD"
NHEADS = H


class Sem:
    def __init__(self, ctx, nc, name):
        self.h = ctx.enter_context(nc.semaphore(name))
        self.n = 0
        self.name = name


class Shadow:

    def __init__(self):
        self.q = {e: [] for e in ("pe", "dve", "act", "sp", "gp")}

    def ev(self, eng, waits, incs):
        self.q[eng].append((list(waits), list(incs)))

    def check(self):
        counts = {}
        pos = {e: 0 for e in self.q}
        total = sum(len(v) for v in self.q.values())
        done = 0
        progress = True
        while done < total:
            if not progress:
                stuck = {e: (self.q[e][pos[e]][0] if pos[e] < len(self.q[e]) else None)
                         for e in self.q}
                raise RuntimeError(f"schedule deadlock; counts={counts}; heads={stuck}")
            progress = False
            for e in self.q:
                while pos[e] < len(self.q[e]):
                    waits, incs = self.q[e][pos[e]]
                    if all(counts.get(s, 0) >= v for s, v in waits):
                        for s, a in incs:
                            counts[s] = counts.get(s, 0) + a
                        pos[e] += 1
                        done += 1
                        progress = True
                    else:
                        break
        return counts


def build_program(stop_phase=None, nheads=None):
    stop_phase = stop_phase or STOP_PHASE
    NH = nheads or NHEADS
    nc = bass.Bass("TRN2", target_bir_lowering=False, debug=False,
                   detect_race_conditions=False)
    sh = Shadow()

    xb = nc.dram_tensor("xb", [T, E], F32, kind="ExternalInput").ap()
    wqkvT = nc.dram_tensor("wqkvT", [128, H, FC, 384], F32, kind="ExternalInput").ap()
    bqkvT = nc.dram_tensor("bqkvT", [128, H, 3], F32, kind="ExternalInput").ap()
    wpT_own = nc.dram_tensor("wpT_own", [1024, E], F32, kind="ExternalInput").ap()
    wpT_oth = nc.dram_tensor("wpT_oth", [1024, E], F32, kind="ExternalInput").ap()
    bp = nc.dram_tensor("bp", [E], F32, kind="ExternalInput").ap()
    wfT = nc.dram_tensor("wfT", [E, E], I8, kind="ExternalInput").ap()
    wsc = nc.dram_tensor("wsc", [E], F32, kind="ExternalInput").ap()
    fbias = nc.dram_tensor("fbias", [E], F32, kind="ExternalInput").ap()
    mAcol = nc.dram_tensor("mAcol", [128, 1], F32, kind="ExternalInput").ap()
    mBcol = nc.dram_tensor("mBcol", [128, 1], F32, kind="ExternalInput").ap()

    out_own = nc.dram_tensor("out_own", [TO, E], F32, kind="ExternalOutput").ap()

    own_dram = nc.dram_tensor("own_dram", [128, H, 1024], BF16, kind="Internal").ap()
    cbufA = nc.dram_tensor("cbufA", [2, 128, 4, 1024], BF16, kind="Internal").ap()
    cbufB = nc.dram_tensor("cbufB", [2, 128, 4, 1024], BF16, kind="Internal").ap()
    rbufA = nc.dram_tensor("rbufA", [2, 128, 4, 1024], BF16, kind="Internal").ap()
    rbufB = nc.dram_tensor("rbufB", [2, 128, 4, 1024], BF16, kind="Internal").ap()
    f16_dram = nc.dram_tensor("f16_dram", [TO, E], F16, kind="Internal").ap()
    rgroups = [[0, 1], [2, 3], [4, 5], [6, 7]]

    ctx = ExitStack()
    sem = lambda name: Sem(ctx, nc, name)

    qq_fm = ctx.enter_context(
        nc.sbuf_tensor("qq_fm", [128, FC, TO], BF16, is_transpose_dest=True))
    isc = ctx.enter_context(nc.sbuf_tensor("isc", [128, NQ], F32))
    mA = ctx.enter_context(nc.sbuf_tensor("mA", [128, 1], F32))
    mB = ctx.enter_context(nc.sbuf_tensor("mB", [128, 1], F32))

    s_xld = sem("xld"); s_sq = sem("sq"); s_var = sem("var"); s_sqv = sem("sqv")
    s_ln = sem("ln"); s_xnb = sem("xnb"); s_tpxn = sem("tpxn")
    s_qq = sem("qq"); s_tpqq = sem("tpqq"); s_msk = sem("msk")
    s_wld = sem("wld"); s_qkmm = sem("qkmm"); s_qkvv = sem("qkvv"); s_vtp = sem("vtp")
    s_smm = sem("smm"); s_sm4 = sem("sm4"); s_spass = sem("spass"); s_nb = sem("nb"); s_expv = sem("expv")
    s_pnv = sem("pnv"); s_rlv = sem("rlv"); s_ptp = [sem("ptp0"), sem("ptp1")]
    s_avm = sem("avm"); s_avev = sem("avev"); s_ownw = sem("ownw"); s_giftw = sem("giftw")
    s_phB = sem("phB")
    s_coll = sem("coll"); s_collA = sem("collA"); s_rld = sem("rld"); s_recvv = sem("recvv"); s_atld = sem("atld")
    s_bc = sem("bc"); s_bc2 = sem("bc2"); s_xotl = sem("xotl")
    s_wpl = sem("wpl"); s_pjm = sem("pjm")
    s_wfl = sem("wfl"); s_fnm = sem("fnm"); s_f1v = sem("f1v"); s_f2v = sem("f2v")
    s_f16w = sem("f16w"); s_f16r = sem("f16r")
    s_oe1 = sem("oe1"); s_outv = sem("outv"); s_outw = sem("outw")
    s_vch = sem("vch")
    s_wfp = sem("wfp"); s_wpl2 = sem("wpl2"); s_wfree = sem("wfree")

    block = ctx.enter_context(nc.Block())

    def wt(eng, ename, *conds):
        for s, v in conds:
            if v > 0:
                eng.wait_ge(s.h, v)
        sh.ev(ename, [(s.name, v) for s, v in conds if v > 0], [])

    def ti(instr, ename, s, amt=1):
        instr.then_inc(s.h, amt)
        sh.ev(ename, [], [(s.name, amt)])

    def dmaS(eng, ename, s, out, in_, transpose=False):
        if s.n > 0:
            eng.wait_ge(s.h, s.n)
        waits = [(s.name, s.n)] if s.n > 0 else []
        if transpose:
            eng.dma_start_transpose(out, in_).then_inc(s.h, 16)
        else:
            eng.dma_start(out=out, in_=in_).then_inc(s.h, 16)
        s.n += 16
        sh.ev(ename, waits, [(s.name, 16)])

    vch_n = [0]

    def vinc(instr):
        vch_n[0] += 1
        instr.then_inc(s_vch.h, 1)
        sh.ev("dve", [], [("vch", 1)])

    def vbar(v):
        v.wait_ge(s_vch.h, vch_n[0])
        sh.ev("dve", [("vch", vch_n[0])], [])

    A_TPXN = 16 * NT
    A_TPQQ = 16 * NQ

    esAB = ExitStack()
    xn_fm = esAB.enter_context(
        nc.sbuf_tensor("xn_fm", [128, FC, T], BF16, is_transpose_dest=True))
    if True:

        esA = ExitStack()
        r0 = esA.enter_context(nc.sbuf_tensor("r0", [128, 1024], F32))
        r1 = esA.enter_context(nc.sbuf_tensor("r1", [128, 1024], F32))
        r2 = esA.enter_context(nc.sbuf_tensor("r2", [128, 1024], F32))
        r3 = esA.enter_context(nc.sbuf_tensor("r3", [128, 1024], F32))
        qqst = esA.enter_context(nc.sbuf_tensor("qqst", [128, 2, E], BF16))
        xtile = esA.enter_context(nc.sbuf_tensor("xtile", [128, 2, E], F32))
        sqscr = esA.enter_context(nc.sbuf_tensor("sqscr", [128, E], F32))
        sxbf = esA.enter_context(nc.sbuf_tensor("sxbf", [128, E], BF16))
        xn_tm = esA.enter_context(nc.sbuf_tensor("xn_tm", [128, E], F32))
        xnbf = esA.enter_context(nc.sbuf_tensor("xnbf", [128, 2, E], BF16))
        sxc = esA.enter_context(nc.sbuf_tensor("sxc", [128, NT], F32))
        sxxc = esA.enter_context(nc.sbuf_tensor("sxxc", [128, NT], F32))
        negm = esA.enter_context(nc.sbuf_tensor("negm", [128, NT], F32))
        varb = esA.enter_context(nc.sbuf_tensor("varb", [128, NT], F32))
        sdev = esA.enter_context(nc.sbuf_tensor("sdev", [128, NT], F32))
        rstd = esA.enter_context(nc.sbuf_tensor("rstd", [128, NT], F32))
        amb = esA.enter_context(nc.sbuf_tensor("amb", [128, 8], F32))
        if True:
            def a_sync(s):
                dmaS(s, "sp", s_msk, mA[:, :], mAcol[:, :])
                dmaS(s, "sp", s_msk, mB[:, :], mBcol[:, :])
                for i in range(NT):
                    if i >= 2:
                        wt(s, "sp", (s_sq, i - 1), (s_ln, i - 1))
                    dmaS(s, "sp", s_xld, xtile[:, i % 2, :],
                         xb[128 * i:128 * (i + 1), :])
                    if i >= 1:
                        j = i - 1
                        wt(s, "sp", (s_xnb, j + 1))
                        dmaS(s, "sp", s_tpxn, xn_fm[:, :, 128 * j:128 * (j + 1)],
                             xnbf[:, j % 2, :], transpose=True)
                wt(s, "sp", (s_xnb, NT))
                dmaS(s, "sp", s_tpxn, xn_fm[:, :, 128 * (NT - 1):128 * NT],
                     xnbf[:, (NT - 1) % 2, :], transpose=True)

            def a_act(sc):
                for i in range(NT + 1):
                    if i >= 1:
                        j = i - 1
                        wt(sc, "act", (s_var, j + 1))
                        ti(sc.activation(sdev[:, j:j + 1], varb[:, j:j + 1], AF.Sqrt,
                                         bias=f32(1e-5), scale=f32(1.0)),
                           "act", s_sqv)
                    if i < NT:
                        wt(sc, "act", (s_xld, 16 * (i + 1)), (s_sq, i))
                        sc.activation(sxbf[:, :], xtile[:, i % 2, :], AF.Identity,
                                      bias=f32(0.0), scale=f32(1.0),
                                      accum_out=sxc[:, i:i + 1])
                        ti(sc.activation(sqscr[:, :], xtile[:, i % 2, :], AF.Square,
                                         bias=f32(0.0), scale=f32(1.0),
                                         accum_out=sxxc[:, i:i + 1]),
                           "act", s_sq)
                    if i >= 1 and i - 1 < NQ:
                        j = i - 1
                        wt(sc, "act", (s_qq, j + 1))
                        dmaS(sc, "act", s_tpqq,
                             qq_fm[:, :, 128 * j:128 * (j + 1)],
                             qqst[:, j % 2, :], transpose=True)

            def a_dve(v):
                for i in range(NT):
                    wt(v, "dve", (s_xld, 16 * (i + 1)), (s_sq, i + 1))
                    vinc(v.tensor_scalar(negm[:, i:i + 1], sxc[:, i:i + 1],
                                         -1.0 / E, None, ALU.mult))
                    v.tensor_scalar(varb[:, i:i + 1], sxxc[:, i:i + 1],
                                    1.0 / E, None, ALU.mult)
                    vbar(v)
                    vinc(v.tensor_mul(amb[:, 4:5], negm[:, i:i + 1],
                                      negm[:, i:i + 1]))
                    vbar(v)
                    ti(v.tensor_sub(varb[:, i:i + 1], varb[:, i:i + 1],
                                    amb[:, 4:5]),
                       "dve", s_var)
                    wt(v, "dve", (s_sqv, i + 1))
                    vinc(v.reciprocal(rstd[:, i:i + 1], sdev[:, i:i + 1]))
                    vbar(v)
                    ti(v.tensor_scalar(xn_tm[:, :], xtile[:, i % 2, :],
                                       negm[:, i:i + 1], rstd[:, i:i + 1],
                                       ALU.add, ALU.mult),
                       "dve", s_ln)
                    if i >= 2:
                        wt(v, "dve", (s_tpxn, 16 * (i - 1)))
                    ti(v.tensor_scalar(xnbf[:, i % 2, :], xn_tm[:, :], 1.0,
                                       None, ALU.mult),
                       "dve", s_xnb)
                    if i < NQ:
                        vinc(v.reduce_max(amb[:, 0:1], xn_tm[:, :], axis=AX.X,
                                          apply_absolute_value=True))
                        vbar(v)
                        vinc(v.tensor_scalar(amb[:, 1:2], amb[:, 0:1], 1e-7,
                                             None, ALU.max))
                        vbar(v)
                        vinc(v.reciprocal(amb[:, 2:3], amb[:, 1:2]))
                        vbar(v)
                        vinc(v.tensor_scalar(amb[:, 3:4], amb[:, 2:3], 127.0,
                                             None, ALU.mult))
                        v.tensor_scalar(isc[:, i:i + 1], amb[:, 1:2],
                                        1.0 / 127.0, None, ALU.mult)
                        vbar(v)
                        if i >= 2:
                            wt(v, "dve", (s_tpqq, 16 * (i - 1)))
                        for hh in range(2):
                            sl = slice(1024 * hh, 1024 * (hh + 1))
                            v.tensor_scalar(r0[:, :], xn_tm[:, sl], amb[:, 3:4],
                                            None, ALU.mult)
                            v.tensor_scalar(r1[:, :], r0[:, :], 256.0, C23,
                                            ALU.add, ALU.add)
                            v.tensor_scalar(r1[:, :], r1[:, :], C23, None,
                                            ALU.subtract)
                            v.tensor_scalar(r0[:, :], r1[:, :], C52, C52,
                                            ALU.add, ALU.subtract)
                            v.tensor_scalar(r2[:, :], r0[:, :], 0.0, None,
                                            ALU.is_equal)
                            v.tensor_mul(r3[:, :], r2[:, :], r1[:, :])
                            qs = v.scalar_tensor_tensor(qqst[:, i % 2, sl],
                                                        r0[:, :], -256.0, r3[:, :],
                                                        ALU.add, ALU.add)
                            if hh == 1:
                                ti(qs, "dve", s_qq)

            block.sync(a_sync)
            block.scalar(a_act)
            block.vector(a_dve)
            esA.close()

        if stop_phase == "A":
            def dbg_out(g):
                wt(g, "gp", (s_ln, NT), (s_tpxn, A_TPXN), (s_tpqq, A_TPQQ))
                for j in range(4):
                    dmaS(g, "gp", s_outw, out_own[128 * j:128 * (j + 1), :],
                         xn_fm[:, j, :])
                for j in range(4):
                    dmaS(g, "gp", s_outw,
                         out_own[128 * (4 + j):128 * (5 + j), 0:1024],
                         qq_fm[:, j, 0:1024])
                for j in range(4):
                    dmaS(g, "gp", s_outw,
                         out_own[128 * (4 + j):128 * (5 + j), 1024:2048],
                         qq_fm[:, 4 + j, 0:1024])
                wt(g, "gp", (s_outw, s_outw.n))
            block.gpsimd(dbg_out)

        if stop_phase != "A":
            esB = ExitStack()
            wqkv_sb = esB.enter_context(nc.sbuf_tensor("wqkv_sb", [128, 2, FC, 384], BF16))
            q_fm = esB.enter_context(nc.sbuf_tensor("q_fm", [128, 2, T], BF16))
            k_fm = esB.enter_context(nc.sbuf_tensor("k_fm", [128, 2, T], BF16))
            v_fm = esB.enter_context(nc.sbuf_tensor("v_fm", [128, 2, T], BF16))
            v_tm = esB.enter_context(nc.sbuf_tensor("v_tm", [128, 2, FC, 128], BF16, is_transpose_dest=True))
            S_sb = esB.enter_context(nc.sbuf_tensor("S_sb", [128, 2, T], F32))
            P_sb = esB.enter_context(nc.sbuf_tensor("P_sb", [128, 2, 2, T], BF16))
            PT = esB.enter_context(nc.sbuf_tensor("PT", [128, 2, 2, FC, 128], BF16, is_transpose_dest=True))
            bqs = esB.enter_context(nc.sbuf_tensor("bqs", [128, 2, 3], F32))
            smax = esB.enter_context(nc.sbuf_tensor("smax", [128, 2], F32))
            m4 = esB.enter_context(nc.sbuf_tensor("m4", [128, 2, 4], F32))
            nbias = esB.enter_context(nc.sbuf_tensor("nbias", [128, 2], F32))
            lrow = esB.enter_context(nc.sbuf_tensor("lrow", [128, 2], F32))
            rlb = esB.enter_context(nc.sbuf_tensor("rlb", [128, 2], F32))
            scrA = esB.enter_context(nc.sbuf_tensor("scrA", [128, 2], F32))
            atst = esB.enter_context(nc.sbuf_tensor("atst", [128, 2, 256], BF16))
            gstA = esB.enter_context(nc.sbuf_tensor("gstA", [128, 2, 256], BF16))
            gstB = esB.enter_context(nc.sbuf_tensor("gstB", [128, 2, 256], BF16))
            S_ps = esB.enter_context(nc.psum_tensor("S_ps", [128, T], F32))
            qkv_ps = esB.enter_context(nc.psum_tensor("qkv_ps", [128, 1024], F32))
            av_ps0 = esB.enter_context(nc.psum_tensor("av_ps0", [128, 512], F32))
            av_ps1 = esB.enter_context(nc.psum_tensor("av_ps1", [128, 512], F32))
            if True:
                av_ps = [av_ps0, av_ps1]

                def rec_phB(e):
                    sh.ev(e, [], [("phB", 1)])

                def wdma(g, h):
                    if h >= 2:
                        wt(g, "gp", (s_qkvv, 6 * (h - 1)))
                    dmaS(g, "gp", s_wld, wqkv_sb[:, h % 2, :, :], wqkvT[:, h, :, :])
                    dmaS(g, "gp", s_wld, bqs[:, h % 2, :], bqkvT[:, h, :])

                def b_gp(g):
                    wt(g, "gp", (s_qq, NQ), (s_tpqq, A_TPQQ))
                    dmaS(g, "gp", s_wld, wqkv_sb[:, 0, :, :], wqkvT[:, 0, :, :])
                    dmaS(g, "gp", s_wld, wqkv_sb[:, 1, :, :], wqkvT[:, 1, :, :])
                    wt(g, "gp", (s_ln, NT), (s_sqv, NT), (s_tpxn, A_TPXN))
                    dmaS(g, "gp", s_wld, bqs[:, 0, :], bqkvT[:, 0, :])
                    dmaS(g, "gp", s_wld, bqs[:, 1, :], bqkvT[:, 1, :])
                    for h in range(NH):
                        if h == 4 and NH == H:
                            cc = g.collective_compute(
                                "AllReduce", ALU.add, rgroups,
                                ins=[cbufA[:, :, :, :]], outs=[rbufA[:, :, :, :]])
                            ti(cc, "gp", s_collA)
                        if h + 2 < NH:
                            wdma(g, h + 2)
                        for gq in range(8):
                            gidx = h * 8 + gq
                            wt(g, "gp", (s_avev, gidx + 1))
                            if gq < 4:
                                dmaS(g, "gp", s_ownw,
                                     own_dram[:, h, 256 * gq:256 * (gq + 1)],
                                     atst[:, gq % 2, :])
                            else:
                                gg = gq - 4
                                cb = cbufA if h < 4 else cbufB
                                hh2 = h % 4
                                dmaS(g, "gp", s_giftw,
                                     cb[0, :, hh2, 256 * gg:256 * (gg + 1)],
                                     gstA[:, gq % 2, :])
                                dmaS(g, "gp", s_giftw,
                                     cb[1, :, hh2, 256 * gg:256 * (gg + 1)],
                                     gstB[:, gq % 2, :])
                    g.engine_nop().then_inc(s_phB.h, 1)
                    rec_phB("gp")

                def qkv_unit(t, h, u, half=None):
                    oc, th = divmod(u, 2)
                    osl = slice(128 * oc, 128 * (oc + 1))
                    toff = 1024 * th
                    halves = (0, 1) if half is None else (half,)
                    uidx = 6 * h + u
                    if halves[0] == 0:
                        wt(t, "pe", (s_wld,
                                     {0: 16, 1: 32}.get(h, 32 * h + 16)))
                        if h == 0:
                            wt(t, "pe", (s_tpxn, 16 * 8 if th == 0 else A_TPXN))
                        if uidx >= 1:
                            wt(t, "pe", (s_qkvv, uidx))
                    for hf in halves:
                        for fc in range(8 * hf, 8 * hf + 8):
                            st = fc == 0
                            sp = fc == FC - 1
                            t.matmul(qkv_ps[:, 0:512], wqkv_sb[:, h % 2, fc, osl],
                                     xn_fm[:, fc, toff:toff + 512],
                                     start=st, stop=sp)
                            mm = t.matmul(qkv_ps[:, 512:1024],
                                          wqkv_sb[:, h % 2, fc, osl],
                                          xn_fm[:, fc, toff + 512:toff + 1024],
                                          start=st, stop=sp)
                    if halves[-1] == 1:
                        ti(mm, "pe", s_qkmm)

                def s_unit(t, h, qt):
                    u = h * NT + qt
                    wt(t, "pe", (s_qkvv, 6 * h + 4))
                    if u >= 1:
                        wt(t, "pe", (s_spass, u))
                    for j in range(4):
                        mm = t.matmul(S_ps[:, 512 * j:512 * (j + 1)],
                                      q_fm[:, h % 2, 128 * qt:128 * (qt + 1)],
                                      k_fm[:, h % 2, 512 * j:512 * (j + 1)],
                                      start=True, stop=True)
                        ti(mm, "pe", s_sm4)

                def av_unit(t, h, gq):
                    gidx = h * 8 + gq
                    wt(t, "pe", (s_ptp[gidx % 2], 16 * (gidx // 2 + 1)),
                       (s_qkvv, 6 * h + 6), (s_vtp, 16 * (h + 1)))
                    if gidx >= 2:
                        wt(t, "pe", (s_avev, gidx - 1))
                    for kc in range(FC):
                        mm = t.matmul(av_ps[gq % 2][:, 0:256], v_tm[:, h % 2, kc, :],
                                      PT[:, gidx % 2, :, kc, :],
                                      start=(kc == 0), stop=(kc == FC - 1))
                    ti(mm, "pe", s_avm)

                def b_pe(t):
                    for h in range(NH):
                        if h == 0:
                            for u in range(6):
                                qkv_unit(t, 0, u)
                        for qt in range(NT):
                            s_unit(t, h, qt)
                            if h + 1 < NH and 1 <= qt <= 12:
                                qkv_unit(t, h + 1, (qt - 1) // 2, half=(qt - 1) % 2)
                            if qt >= 6 and qt % 2 == 0:
                                av_unit(t, h, (qt - 6) // 2)
                        for gq in (5, 6, 7):
                            av_unit(t, h, gq)
                    t.nop().then_inc(s_phB.h, 1)
                    rec_phB("pe")

                def qkv_evac(v, h, u):
                    oc, th = divmod(u, 2)
                    toff = 1024 * th
                    dst = [q_fm, k_fm, v_fm][oc]
                    uidx = 6 * h + u
                    wt(v, "dve", (s_qkmm, uidx + 1),
                       (s_wld, {0: 48, 1: 64}.get(h, 32 * h + 32)))
                    if h >= 2:
                        if oc < 2:
                            wt(v, "dve", (s_sm4, 64 * (h - 1)))
                        else:
                            wt(v, "dve", (s_vtp, 16 * (h - 1)))
                    ti(v.tensor_scalar(dst[:, h % 2, toff:toff + 1024], qkv_ps[:, :],
                                       bqs[:, h % 2, oc:oc + 1], None, ALU.add),
                       "dve", s_qkvv)

                def s_pass(v, h, qt):
                    u = h * NT + qt
                    if u >= 2:
                        wt(v, "dve", (s_expv, u - 1))
                    for j in range(4):
                        wt(v, "dve", (s_sm4, 4 * u + j + 1))
                        tsj = v.tensor_scalar(S_sb[:, qt % 2, 512 * j:512 * (j + 1)],
                                              S_ps[:, 512 * j:512 * (j + 1)], C_SCALE,
                                              None, ALU.mult, ALU.max,
                                              accum_out=m4[:, qt % 2, j:j + 1])
                        if j == 3:
                            vinc(tsj)
                    vbar(v)
                    ti(v.reduce_max(nbias[:, qt % 2:qt % 2 + 1],
                                    m4[:, qt % 2, :], axis=AX.X, negate=True),
                       "dve", s_spass)


                def rl_pass(v, u):
                    qt = u % NT
                    wt(v, "dve", (s_expv, u + 1))
                    ti(v.reciprocal(rlb[:, qt % 2:qt % 2 + 1],
                                    lrow[:, qt % 2:qt % 2 + 1]),
                       "dve", s_rlv)

                def av_evac(v, h, gq):
                    gidx = h * 8 + gq
                    wt(v, "dve", (s_avm, gidx + 1))
                    if gq < 4:
                        own_idx = 4 * h + gq
                        if own_idx >= 2:
                            wt(v, "dve", (s_ownw, 16 * (own_idx - 1)))
                        ti(v.tensor_copy(atst[:, gq % 2, :], av_ps[gq % 2][:, 0:256]),
                           "dve", s_avev)
                    else:
                        gidx2 = 4 * h + (gq - 4)
                        if gidx2 >= 2:
                            wt(v, "dve", (s_giftw, 32 * (gidx2 - 1)))
                        v.tensor_scalar(gstA[:, gq % 2, :], av_ps[gq % 2][:, 0:256],
                                        mA[:, 0:1], None, ALU.mult)
                        ti(v.tensor_scalar(gstB[:, gq % 2, :],
                                           av_ps[gq % 2][:, 0:256],
                                           mB[:, 0:1], None, ALU.mult),
                           "dve", s_avev)

                def b_dve(v):
                    wt(v, "dve", (s_msk, 32))
                    for h in range(NH):
                        if h == 0:
                            for u in range(6):
                                qkv_evac(v, 0, u)
                        for qt in range(NT):
                            s_pass(v, h, qt)
                            u = h * NT + qt
                            if qt >= 1:
                                rl_pass(v, u - 1)
                            if h + 1 < NH and qt in (3, 5, 7, 9, 11, 13):
                                qkv_evac(v, h + 1, (qt - 3) // 2)
                            if qt >= 7 and qt % 2 == 1:
                                av_evac(v, h, (qt - 7) // 2)
                        rl_pass(v, h * NT + NT - 1)
                        for gq in (5, 6, 7):
                            av_evac(v, h, gq)
                    v.engine_nop().then_inc(s_phB.h, 1)
                    rec_phB("dve")

                def pnorm(sc, u):
                    qt = u % NT
                    wt(sc, "act", (s_rlv, u + 1))
                    ti(sc.mul(P_sb[:, (qt // 2) % 2, qt % 2, :],
                              P_sb[:, (qt // 2) % 2, qt % 2, :],
                              rlb[:, qt % 2:qt % 2 + 1]),
                       "act", s_pnv)

                def b_act(sc):
                    sc.memzero(scrA[:, :])
                    for h in range(NH):
                        for qt in range(NT):
                            u = h * NT + qt
                            gidx = h * 8 + qt // 2
                            wt(sc, "act", (s_spass, u + 1))
                            if gidx >= 2:
                                wt(sc, "act", (s_ptp[gidx % 2], 16 * (gidx // 2)))

                            ti(sc.activation(P_sb[:, (qt // 2) % 2, qt % 2, :],
                                             S_sb[:, qt % 2, :], AF.Exp,
                                             bias=nbias[:, qt % 2:qt % 2 + 1],
                                             scale=f32(1.0),
                                             accum_out=lrow[:, qt % 2:qt % 2 + 1]),
                               "act", s_expv)
                            if qt >= 1:
                                pnorm(sc, u - 1)
                        pnorm(sc, h * NT + NT - 1)
                    ti(sc.copy(scrA[:, 1:2], scrA[:, 0:1]), "act", s_phB)

                def b_sync(s):
                    for h in range(NH):
                        wt(s, "sp", (s_qkvv, 6 * h + 6))
                        if h >= 2:
                            wt(s, "sp", (s_avm, 8 * (h - 1)))
                        dmaS(s, "sp", s_vtp, v_tm[:, h % 2, :, :],
                             v_fm[:, h % 2, :], transpose=True)
                        for gq in range(8):
                            gidx = h * 8 + gq
                            wt(s, "sp", (s_pnv, h * NT + 2 * gq + 2))
                            if gidx >= 2:
                                wt(s, "sp", (s_avm, gidx - 1))
                            dmaS(s, "sp", s_ptp[gidx % 2], PT[:, gidx % 2, :, :, :],
                                 P_sb[:, gq % 2, :, :].rearrange(
                                     "p a t -> p (a t)"), transpose=True)
                    s.sem_inc(s_phB.h, 1)
                    rec_phB("sp")

                block.gpsimd(b_gp)
                block.tensor(b_pe)
                block.vector(b_dve)
                block.scalar(b_act)
                block.sync(b_sync)
                esB.close()

    esAB.close()

    if stop_phase != "CD":
        if stop_phase == "B":
            def dbg_out2(s):
                wt(s, "sp", (s_phB, 5))
                for j in range(NQ):
                    dmaS(s, "sp", s_outw, out_own[128 * j:128 * (j + 1), :],
                         xb[128 * j:128 * (j + 1), :])
                wt(s, "sp", (s_outw, 16 * NQ))
            block.sync(dbg_out2)
        sh.check()
        ctx.close()
        return nc

    B_OWNW = 16 * 4 * NH
    B_GIFTW = 32 * 4 * NH
    B_SPASS = NH * NT
    B_AVEV = NH * 8
    B_QKVV = NH * 6

    esCD = ExitStack()
    at_sb = esCD.enter_context(nc.sbuf_tensor("at_sb", [128, H, 1024], BF16))
    r0_sb = esCD.enter_context(nc.sbuf_tensor("r0_sb", [128, H, 1024], BF16))
    w_sb = esCD.enter_context(nc.sbuf_tensor("w_sb", [128, FC, E], BF16))
    xot = esCD.enter_context(nc.sbuf_tensor("xot", [128, 2, E], F32))
    wbb = esCD.enter_context(nc.sbuf_tensor("wbb", [128, E], F32))
    bfb = esCD.enter_context(nc.sbuf_tensor("bfb", [128, E], F32))
    f16st = esCD.enter_context(nc.sbuf_tensor("f16st", [128, 2, E], F16))
    f16r = esCD.enter_context(nc.sbuf_tensor("f16r", [128, 2, E], F16))
    osb = esCD.enter_context(nc.sbuf_tensor("osb", [128, 2, E], F32))
    mm_ps0 = esCD.enter_context(nc.psum_tensor("mm_ps0", [128, E], F32))
    mm_ps1 = esCD.enter_context(nc.psum_tensor("mm_ps1", [128, E], F32))
    if True:
        mm_ps = [mm_ps0, mm_ps1]

        def cd_gp(g):
            wt(g, "gp", (s_giftw, B_GIFTW), (s_ownw, B_OWNW))
            cc = g.collective_compute("AllReduce", ALU.add, rgroups,
                                      ins=[cbufB[:, :, :, :]],
                                      outs=[rbufB[:, :, :, :]])
            ti(cc, "gp", s_coll)
            wt(g, "gp", (s_phB, 5))
            for ec in range(FC):
                dmaS(g, "gp", s_wfl if ec % 2 == 0 else s_wfp, w_sb[:, ec, :],
                     wfT[128 * ec:128 * (ec + 1), :])
            wt(g, "gp", (s_collA, 1))
            dmaS(g, "gp", s_rld, r0_sb[:, 0:4, :], rbufA[0, :, :, :])
            dmaS(g, "gp", s_rld, at_sb[:, 0:4, :], rbufA[1, :, :, :])
            wt(g, "gp", (s_coll, 1))
            dmaS(g, "gp", s_rld, r0_sb[:, 4:8, :], rbufB[0, :, :, :])
            dmaS(g, "gp", s_rld, at_sb[:, 4:8, :], rbufB[1, :, :, :])
            wt(g, "gp", (s_recvv, 2))
            dmaS(g, "gp", s_atld, at_sb[:, :, :], own_dram[:, :, :])
            for ec in range(FC):
                if ec < FC - 1:
                    wt(g, "gp", (s_wfree, ec + 1))
                else:
                    wt(g, "gp", (s_fnm, NQ))
                wsrc = wpT_own if ec < H else wpT_oth
                r = ec % H
                dmaS(g, "gp", s_wpl if ec % 2 == 0 else s_wpl2, w_sb[:, ec, :],
                     wsrc[128 * r:128 * (r + 1), :])


        def cd_dve(v):
            for j in range(NQ):
                wt(v, "dve", (s_fnm, j + 1), (s_bc, 32))
                if j >= 2:
                    wt(v, "dve", (s_f16w, 16 * (j - 1)))
                ti(v.scalar_tensor_tensor(osb[:, j % 2, :], mm_ps[j % 2][:, :],
                                          isc[:, j:j + 1], wbb[:, :],
                                          ALU.mult, ALU.mult),
                   "dve", s_f1v)
                ti(v.tensor_add(f16st[:, j % 2, :], osb[:, j % 2, :], bfb[:, :]),
                   "dve", s_f2v)
            wt(v, "dve", (s_rld, 32))
            v.tensor_scalar(r0_sb[:, 0:4, :], r0_sb[:, 0:4, :], mB[:, 0:1],
                            None, ALU.mult)
            ti(v.scalar_tensor_tensor(r0_sb[:, 0:4, :], at_sb[:, 0:4, :], mA[:, 0:1],
                                      r0_sb[:, 0:4, :], ALU.mult, ALU.add),
               "dve", s_recvv)
            wt(v, "dve", (s_rld, 64))
            v.tensor_scalar(r0_sb[:, 4:8, :], r0_sb[:, 4:8, :], mB[:, 0:1],
                            None, ALU.mult)
            ti(v.scalar_tensor_tensor(r0_sb[:, 4:8, :], at_sb[:, 4:8, :], mA[:, 0:1],
                                      r0_sb[:, 4:8, :], ALU.mult, ALU.add),
               "dve", s_recvv)
            for j in range(NQ):
                wt(v, "dve", (s_pjm, j + 1), (s_xotl, 16 * (j + 1)),
                   (s_f16w, 16 * (j + 1)))
                if j >= 2:
                    wt(v, "dve", (s_outw, 16 * (j - 1)))
                wt(v, "dve", (s_f16r, 16 * (j + 1)), (s_bc2, 16))
                ti(v.scalar_tensor_tensor(osb[:, j % 2, :], mm_ps[j % 2][:, :], 0.0,
                                          xot[:, j % 2, :], ALU.add, ALU.add),
                   "dve", s_oe1)
                v.tensor_add(osb[:, j % 2, :], osb[:, j % 2, :], wbb[:, :])
                ti(v.tensor_add(osb[:, j % 2, :], osb[:, j % 2, :],
                                f16r[:, j % 2, :]),
                   "dve", s_outv)

        def cd_pe(t):
            for j in range(NQ):
                wt(t, "pe", (s_phB, 5), (s_spass, B_SPASS), (s_avev, B_AVEV),
                   (s_qkvv, B_QKVV))
                if j >= 2:
                    wt(t, "pe", (s_f1v, j - 1))
                for ec in range(FC):
                    wt(t, "pe", (s_wfl if ec % 2 == 0 else s_wfp,
                                 16 * (ec // 2 + 1)))
                    st = ec == 0
                    sp = ec == FC - 1
                    for oq in range(4):
                        mm = t.matmul(mm_ps[j % 2][:, 512 * oq:512 * (oq + 1)],
                                      qq_fm[:, ec, 128 * j:128 * (j + 1)],
                                      w_sb[:, ec, 512 * oq:512 * (oq + 1)],
                                      start=st, stop=sp)
                    if j == NQ - 1 and ec < FC - 1:
                        ti(mm, "pe", s_wfree)
                ti(mm, "pe", s_fnm)
            for j in range(NQ):
                wt(t, "pe", (s_recvv, 2), (s_atld, 16), (s_f1v, j + 1))
                if j >= 2:
                    wt(t, "pe", (s_oe1, j - 1))
                for ec in range(FC):
                    wt(t, "pe", (s_wpl if ec % 2 == 0 else s_wpl2,
                                 16 * (ec // 2 + 1)))
                    lhs = (at_sb[:, ec, 128 * j:128 * (j + 1)] if ec < H
                           else r0_sb[:, ec - H, 128 * j:128 * (j + 1)])
                    st = ec == 0
                    sp = ec == FC - 1
                    for oq in range(4):
                        mm = t.matmul(mm_ps[j % 2][:, 512 * oq:512 * (oq + 1)],
                                      lhs, w_sb[:, ec, 512 * oq:512 * (oq + 1)],
                                      start=st, stop=sp)
                ti(mm, "pe", s_pjm)

        def cd_sync(s):
            wt(s, "sp", (s_phB, 5))
            dmaS(s, "sp", s_bc, wbb[:, :], wsc[None, :].broadcast_to((128, E)))
            dmaS(s, "sp", s_bc, bfb[:, :], fbias[None, :].broadcast_to((128, E)))
            for j in range(NQ):
                wt(s, "sp", (s_f2v, j + 1))
                dmaS(s, "sp", s_f16w, f16_dram[128 * j:128 * (j + 1), :],
                     f16st[:, j % 2, :])
            wt(s, "sp", (s_f1v, NQ))
            dmaS(s, "sp", s_bc2, wbb[:, :], bp[None, :].broadcast_to((128, E)))
            for j in range(NQ + 2):
                if j < NQ:
                    if j >= 2:
                        wt(s, "sp", (s_oe1, j - 1))
                    dmaS(s, "sp", s_xotl, xot[:, j % 2, :],
                         xb[128 * j:128 * (j + 1), :])
                    wt(s, "sp", (s_f16w, 16 * (j + 1)))
                    if j >= 2:
                        wt(s, "sp", (s_outv, j - 1))
                    dmaS(s, "sp", s_f16r, f16r[:, j % 2, :],
                         f16_dram[128 * j:128 * (j + 1), :])
                if j >= 2:
                    jj = j - 2
                    wt(s, "sp", (s_outv, jj + 1))
                    dmaS(s, "sp", s_outw, out_own[128 * jj:128 * (jj + 1), :],
                         osb[:, jj % 2, :])
            wt(s, "sp", (s_outw, 16 * NQ))

        block.gpsimd(cd_gp)
        block.vector(cd_dve)
        block.tensor(cd_pe)
        block.sync(cd_sync)
        esCD.close()

    sh.check()
    ctx.close()
    return nc


_NC_CACHE = {}


def _get_nc():
    if "nc" not in _NC_CACHE:
        _NC_CACHE["nc"] = build_program()
    return _NC_CACHE["nc"]


def kernel(L0_attn_query, L0_attn_key, L0_attn_value, L0_attn_Wqkv, L0_attn_bqkv,
           L0_attn_Wproj, L0_attn_bproj, L0_ffn_weight, L0_ffn_w_scale, L0_ffn_bias):
    x = np.asarray(L0_attn_query)
    Wqkv = np.asarray(L0_attn_Wqkv)
    bqkv = np.asarray(L0_attn_bqkv)
    Wproj = np.asarray(L0_attn_Wproj)
    bproj = np.asarray(L0_attn_bproj)
    wf = np.asarray(L0_ffn_weight)
    wscale = np.asarray(L0_ffn_w_scale)
    fb = np.asarray(L0_ffn_bias)

    Wr = Wqkv.reshape(16, 384, 2048)
    br = bqkv.reshape(16, 384)
    WpT = np.ascontiguousarray(Wproj.T)
    wfT = np.ascontiguousarray(wf.T)

    in_maps = []
    for c in range(8):
        b, g = divmod(c, 2)
        own = x[b, 1024 * g:1024 * (g + 1)]
        oth = x[b, 1024 * (1 - g):1024 * (2 - g)]
        xb_rot = np.ascontiguousarray(np.concatenate([own, oth], axis=0))
        wh = Wr[8 * g:8 * g + 8]
        wqkvT_c = np.ascontiguousarray(
            wh.reshape(8, 384, FC, 128).transpose(3, 0, 2, 1))
        bq_c = np.ascontiguousarray(
            br[8 * g:8 * g + 8].reshape(8, 3, 128).transpose(2, 0, 1))
        in_maps.append({
            "xb": xb_rot,
            "wqkvT": wqkvT_c,
            "bqkvT": bq_c,
            "wpT_own": np.ascontiguousarray(WpT[1024 * g:1024 * (g + 1)]),
            "wpT_oth": np.ascontiguousarray(WpT[1024 * (1 - g):1024 * (2 - g)]),
            "bp": bproj,
            "wfT": wfT,
            "wsc": np.ascontiguousarray(wscale[:, 0]),
            "fbias": fb,
            "mAcol": np.full((128, 1), 1.0 if g == 0 else 0.0, np.float32),
            "mBcol": np.full((128, 1), 0.0 if g == 0 else 1.0, np.float32),
        })

    nc = _get_nc()
    res = run_bass_kernel_spmd(nc, in_maps, core_ids=list(range(8)))

    out = np.empty((4, 2048, 2048), np.float32)
    for c in range(8):
        b, g = divmod(c, 2)
        out[b, 1024 * g:1024 * (g + 1)] = res.results[c]["out_own"]
    return out
